# version 01
# baseline (speedup 1.0000x reference)
import math
import numpy as np
import ml_dtypes
import concourse.bass as bass
import concourse.mybir as mybir
from concourse.bass_utils import run_bass_kernel_spmd

F32 = mybir.dt.float32
BF16 = mybir.dt.bfloat16
AF = mybir.ActivationFunctionType
ALU = mybir.AluOpType
AX = mybir.AxisListType
BF = ml_dtypes.bfloat16

S, D, FF = 8192, 4096, 16384
NCORE = 8
TPC = 1024
CH = 512
H = 32
QHD = 128
KVR = 256
C_SHIFT = 25.0
NEG = -1.0e30
LN_EPS = 1e-5
RMS_EPS = 1e-6
SCALE = (QHD ** -0.5) * (0.1 * math.log(128.0) + 1.0) ** 2


class Prog:
    ENGS = ("te", "se", "ve", "sy", "gp")
    SERIAL = {"ve": "vS", "se": "sS"}

    def __init__(self, nc):
        self.nc = nc
        self.q = {k: [] for k in self.ENGS}
        self._sems = {}
        self.virt = {}

    def sem(self, name):
        if name not in self._sems:
            self._sems[name] = [self.nc.alloc_semaphore(name), 0]
        return self._sems[name]

    def bump(self, name, amt):
        s = self.sem(name)
        s[1] += amt
        return s[1]

    def val(self, name):
        if name in self.virt:
            return len(self.virt[name][1])
        return self.sem(name)[1]

    def _xlate(self, n, v):
        if v <= 0:
            return None
        if n in self.virt:
            seqname, vals = self.virt[n]
            return (seqname, vals[v - 1])
        return (n, v)

    def emit(self, eng, mk, waits=(), inc=None, inc_amt=None):
        wl0 = []
        for (n, v) in waits:
            t = self._xlate(n, v)
            if t is not None:
                wl0.append(t)
        if eng in self.SERIAL:
            seqname = self.SERIAL[eng]
            cur = self.val(seqname)
            if cur > 0:
                wl0.append((seqname, cur))
            if inc is not None:
                assert inc_amt in (None, 1)
                vals = self.virt.setdefault(inc, (seqname, (lambda: [])()))
                assert vals[0] == seqname, (inc, vals[0], seqname)
                vals[1].append(cur + 1)
            tgt_name, amt = seqname, 1
            self.bump(seqname, 1)
        elif inc is not None:
            amt = 1 if inc_amt is None else inc_amt
            tgt_name = inc
            self.bump(inc, amt)
        else:
            tgt_name, amt = None, 0
        wd = {}
        for (n, v) in wl0:
            wd[n] = max(wd.get(n, 0), v)
        wl = [(self.sem(n)[0], v) for n, v in wd.items()]
        tgt = self.sem(tgt_name)[0] if tgt_name is not None else None

        def fn(e, wl=wl, mk=mk, tgt=tgt, amt=amt):
            for s, v in wl:
                e.wait_ge(s, v)
            ins = mk(e)
            if tgt is not None:
                ins.then_inc(tgt, amt)
        self.q[eng].append(fn)

    def wait(self, eng, name, value=None):
        v = self.val(name) if value is None else value
        t = self._xlate(name, v)
        if t is None:
            return
        n, v = t
        s = self.sem(n)[0]
        self.q[eng].append(lambda e, s=s, v=v: e.wait_ge(s, v))

    def run(self):
        nc = self.nc
        with nc.Block() as blk:
            for eng, reg in (("gp", blk.gpsimd), ("te", blk.tensor),
                             ("se", blk.scalar), ("ve", blk.vector),
                             ("sy", blk.sync)):
                fns = self.q[eng]

                def body(e, fns=fns):
                    for f in fns:
                        f(e)
                reg(body)


PHASES = 99
DBG = False
SUB = 'c'


def _finish(p, nc):
    for name, (h, v) in list(p._sems.items()):
        if name.startswith("d") and v > 0:
            p.wait("sy", name, v)
    p.run()
    return nc


def build_l1():
    nc = bass.Bass("TRN2", target_bir_lowering=False, detect_race_conditions=False)
    p = Prog(nc)

    x_in = nc.dram_tensor("x", [TPC, D], F32, kind="ExternalInput").ap()
    w_in = nc.dram_tensor("w_in", [D, FF], F32, kind="ExternalInput").ap()
    binT_in = nc.dram_tensor("binT", [128, 128], F32, kind="ExternalInput").ap()
    w_out = nc.dram_tensor("w_out", [FF, D], F32, kind="ExternalInput").ap()
    boutT_in = nc.dram_tensor("boutT", [128, 32], F32, kind="ExternalInput").ap()
    wqaT_in = nc.dram_tensor("wqaT", [D, 1024], F32, kind="ExternalInput").ap()
    wqbT_in = nc.dram_tensor("wqbT", [1024, D], F32, kind="ExternalInput").ap()
    wlnq_in = nc.dram_tensor("wlnq", [128, 8], F32, kind="ExternalInput").ap()
    wkvaT_in = nc.dram_tensor("wkvaT", [D, 320], F32, kind="ExternalInput").ap()
    cos_in = nc.dram_tensor("cosT", [64, TPC], F32, kind="ExternalInput").ap()
    sin_in = nc.dram_tensor("sinT", [64, TPC], F32, kind="ExternalInput").ap()
    iden_in = nc.dram_tensor("iden", [128, 128], F32, kind="ExternalInput").ap()

    y1_out = nc.dram_tensor("y1F", [D, TPC], F32, kind="ExternalOutput").ap()
    con_out = nc.dram_tensor("contrib", [320, TPC], F32, kind="ExternalOutput").ap()
    qf_out = nc.dram_tensor("qF", [D, TPC], BF16, kind="ExternalOutput").ap()

    xnt_d = nc.dram_tensor("xnt_d", [TPC, D], BF16, kind="Internal").ap()
    xf_d = nc.dram_tensor("xf_d", [D, TPC], F32, kind="Internal").ap()
    hsp_d = nc.dram_tensor("hsp_d", [FF, TPC], BF16, kind="Internal").ap()
    rows_d = nc.dram_tensor("rows_d", [4, TPC], F32, kind="Internal").ap()

    arA = nc.alloc_sbuf_tensor("arA", [128, 32768], BF16)
    arB = nc.alloc_sbuf_tensor("arB", [128, 32768], BF16)
    arC = nc.alloc_sbuf_tensor("arC", [128, 16384], BF16)
    A32 = arA[:].bitcast(F32)
    B32 = arB[:].bitcast(F32)
    C32 = arC[:].bitcast(F32)

    iden = nc.alloc_sbuf_tensor("idenS", [128, 128], F32)
    ones = nc.alloc_sbuf_tensor("onesS", [128, 128], F32)
    binT = nc.alloc_sbuf_tensor("binTS", [128, 128], F32)
    bhalf = nc.alloc_sbuf_tensor("bhalfS", [128, 32], F32)
    wlnq = nc.alloc_sbuf_tensor("wlnqS", [128, 8], F32)
    wlnqs = nc.alloc_sbuf_tensor("wlnqsS", [128, 8], F32)
    AUX = nc.alloc_sbuf_tensor("auxS", [64, 4096], F32)
    cosT = AUX[:, 0:1024]
    sinT = AUX[:, 1024:2048]
    RA = AUX[0:1, 2048:3072]
    RB = AUX[0:1, 3072:4096]
    stat = nc.alloc_sbuf_tensor("statS", [128, 32], F32)
    mb = nc.alloc_sbuf_tensor("mbS", [128, 1024], F32)
    rb = nc.alloc_sbuf_tensor("rbS", [128, 1024], F32)
    xfbuf = nc.alloc_sbuf_tensor("xfbufS", [128, 2048], F32)
    y1t = nc.alloc_sbuf_tensor("y1tS", [128, 1024], F32)
    y1sq = nc.alloc_sbuf_tensor("y1sqS", [128, 1024], F32)
    hst = y1t[:].bitcast(BF16)
    tstage = xfbuf[:, 0:1024]
    qstage = mb[:].bitcast(BF16)
    ck0 = nc.alloc_sbuf_tensor("ck0S", [128, 1024], F32)
    ck1 = y1sq[:]
    kpe = xfbuf[0:64, 0:1024]
    kpr = xfbuf[0:64, 1024:2048]
    rop1 = xfbuf[64:96, 0:1024]
    rop2 = xfbuf[64:96, 1024:2048]

    PS = [nc.alloc_psum_tensor(f"ps{i}", [128, 512], F32) for i in range(4)]
    PST = [nc.alloc_psum_tensor(f"pst{i}", [1, 512], F32) for i in range(4)]

    p.emit("sy", lambda e: e.dma_start(iden[:], iden_in), inc="dC", inc_amt=16)
    p.emit("sy", lambda e: e.dma_start(binT[:], binT_in), inc="dC", inc_amt=16)
    p.emit("sy", lambda e: e.dma_start(bhalf[:], boutT_in), inc="dC", inc_amt=16)
    p.emit("sy", lambda e: e.dma_start(wlnq[:], wlnq_in), inc="dC", inc_amt=16)
    p.emit("sy", lambda e: e.dma_start(cosT, cos_in), inc="dC", inc_amt=16)
    p.emit("sy", lambda e: e.dma_start(sinT, sin_in), inc="dC", inc_amt=16)
    dC_all = p.val("dC")
    p.emit("ve", lambda e: e.memset(ones[:], 1.0), inc="vC")
    p.emit("ve", lambda e: e.tensor_scalar(bhalf[:], bhalf[:], 0.5, None, op0=ALU.mult),
           waits=[("dC", dC_all)], inc="vC")
    p.emit("ve", lambda e: e.tensor_scalar(wlnqs[:], wlnq[:], float(SCALE), None, op0=ALU.mult),
           waits=[("dC", dC_all)], inc="vC")
    vC_all = p.val("vC")

    def xe_ap(t):
        base = A32 if t < 4 else B32
        return base[:, (t % 4) * 4096:(t % 4 + 1) * 4096]

    for t in range(8):
        p.emit("sy", lambda e, t=t: e.dma_start(xe_ap(t), x_in[t * 128:(t + 1) * 128, :]),
               inc="dXin", inc_amt=16)

    xsq = C32[:, 0:4096]
    xnt_slots = [arC[:, 8192:12288], arC[:, 12288:16384]]
    for t in range(8):
        p.emit("ve", lambda e, t=t: e.reduce_sum(stat[:, 4 * t:4 * t + 1], xe_ap(t), axis=AX.X),
               waits=[("dXin", 16 * 8)], inc="vE")
        p.emit("se", lambda e, t=t: e.activation(xsq, xe_ap(t), AF.Square),
               waits=[("dXin", 16 * 8), ("vSq", p.val("vSq"))], inc="sE")
        p.emit("ve", lambda e, t=t: e.reduce_sum(stat[:, 4 * t + 1:4 * t + 2], xsq, axis=AX.X),
               waits=[("sE", p.val("sE"))], inc="vSq")
        c0 = stat[:, 4 * t:4 * t + 1]
        c1 = stat[:, 4 * t + 1:4 * t + 2]
        c2 = stat[:, 4 * t + 2:4 * t + 3]
        c3 = stat[:, 4 * t + 3:4 * t + 4]
        p.emit("ve", lambda e, c0=c0, c2=c2: e.tensor_scalar(c2, c0, 1.0 / D, None, op0=ALU.mult),
               waits=[("vE", p.val("vE"))], inc="vE")
        p.emit("ve", lambda e, c0=c0, c2=c2: e.tensor_tensor(c0, c2, c2, op=ALU.mult), inc="vE")
        p.emit("ve", lambda e, c1=c1, c3=c3: e.tensor_scalar(c3, c1, 1.0 / D, LN_EPS,
                                                             op0=ALU.mult, op1=ALU.add),
               waits=[("vSq", p.val("vSq"))], inc="vE")
        p.emit("ve", lambda e, c0=c0, c3=c3: e.tensor_tensor(c0, c3, c0, op=ALU.subtract), inc="vE")
        vstat = p.val("vE")
        p.emit("se", lambda e, c0=c0, c3=c3: e.activation(c3, c0, AF.Ln),
               waits=[("vE", vstat)], inc="sE")
        p.emit("ve", lambda e, c3=c3: e.tensor_scalar(c3, c3, -0.5, None, op0=ALU.mult),
               waits=[("sE", p.val("sE"))], inc="vE")
        p.emit("se", lambda e, c3=c3: e.activation(c3, c3, AF.Exp),
               waits=[("vE", p.val("vE"))], inc="sE")
        sr = p.val("sE")
        xnt_t = xnt_slots[t % 2]
        sem_x = "dXnt%d" % (t % 2)
        p.emit("ve", lambda e, t=t, xnt_t=xnt_t: e.tensor_scalar(xnt_t, xe_ap(t),
                                                    stat[:, 4 * t + 2:4 * t + 3],
                                                    stat[:, 4 * t + 3:4 * t + 4],
                                                    op0=ALU.subtract, op1=ALU.mult),
               waits=[("sE", sr), (sem_x, p.val(sem_x))], inc="vXn")
        p.emit("sy", lambda e, t=t, xnt_t=xnt_t: e.dma_start(xnt_d[t * 128:(t + 1) * 128, :], xnt_t),
               waits=[("vXn", p.val("vXn"))], inc=sem_x, inc_amt=16)

    nT = 0
    tslots = [xfbuf[:, 0:1024], xfbuf[:, 1024:2048]]
    for k in range(32):
        ts = tslots[k % 2]
        sem_f = "dXf%d" % (k % 2)
        for t in range(8):
            bank = PS[nT % 2][:, 0:128]
            waits = [("dXin", 16 * 8), ("dC", dC_all)]
            if nT >= 2:
                waits.append(("vT", nT - 1))
            p.emit("te", lambda e, k=k, t=t, bank=bank: e.transpose(
                bank, xe_ap(t)[:, k * 128:(k + 1) * 128], iden[:]),
                waits=waits, inc="tT")
            vw = [("tT", p.val("tT"))]
            if t == 0:
                vw.append((sem_f, p.val(sem_f)))
            p.emit("ve", lambda e, t=t, bank=bank, ts=ts: e.tensor_copy(
                ts[:, t * 128:(t + 1) * 128], bank),
                waits=vw, inc="vT")
            nT += 1
        p.emit("sy", lambda e, k=k, ts=ts: e.dma_start(xf_d[k * 128:(k + 1) * 128, :], ts),
               waits=[("vT", p.val("vT"))], inc=sem_f, inc_amt=16)
    dXf_tot = ("dXf0", p.val("dXf0")), ("dXf1", p.val("dXf1"))

    teT_all = p.val("tT")

    if PHASES < 2:
        return _finish(p, nc)
    xn1 = arA[:]
    for k in range(32):
        p.emit("sy", lambda e, k=k: e.dma_start(
            xn1[:, k * 1024:(k + 1) * 1024],
            xnt_d[:, k * 128:(k + 1) * 128], transpose=True),
            waits=[("dXnt0", p.val("dXnt0")), ("dXnt1", p.val("dXnt1")),
                   ("tT", teT_all)], inc="dXn1", inc_amt=16)
    dXn1_all = p.val("dXn1")

    if PHASES < 3:
        return _finish(p, nc)
    winb = arB[:]
    for w in range(32):
        slot = w % 2
        sem_w = "dWin%d" % slot
        dst = winb[:, slot * 16384:(slot + 1) * 16384].rearrange("p (k f) -> p k f", k=32)
        src = w_in.rearrange("(k p) f -> p k f", p=128)[:, :, w * 512:(w + 1) * 512]
        waits = [("dXn1", dXn1_all)]
        if w >= 2:
            waits.append(("tF1", 8 * (w - 1)))
        p.emit("gp", lambda e, dst=dst, src=src: e.dma_start(dst, src),
               waits=waits, inc=sem_w, inc_amt=16)
        for j in range(4):
            fidx = w * 4 + j
            for T in range(2):
                bank = PS[(fidx * 2 + T) % 4]
                mwaits = [(sem_w, p.val(sem_w))]
                if fidx * 2 + T >= 4:
                    mwaits.append(("sF1", fidx * 2 + T - 3))

                def mk(e, slot=slot, j=j, T=T, bank=bank):
                    lb = winb[:, slot * 16384:(slot + 1) * 16384].rearrange(
                        "p (k f) -> p k f", k=32)
                    for k in range(32):
                        ins = e.matmul(bank[:],
                                       lb[:, k, j * 128:(j + 1) * 128],
                                       xn1[:, k * 1024 + T * 512: k * 1024 + T * 512 + 512],
                                       start=(k == 0), stop=(k == 31))
                    return ins
                p.emit("te", mk, waits=mwaits, inc="tF1")
                sem_h = "dHsp%d" % (fidx % 2)
                hw = [("tF1", p.val("tF1"))]
                if fidx >= 2:
                    hw.append((sem_h, p.val(sem_h)))
                p.emit("se", lambda e, fidx=fidx, T=T, bank=bank: e.activation(
                    hst[:, (fidx % 2) * 1024 + T * 512: (fidx % 2) * 1024 + T * 512 + 512],
                    bank[:], AF.Gelu_apprx_tanh, bias=binT[:, fidx:fidx + 1]),
                    waits=hw, inc="sF1")
            p.emit("sy", lambda e, fidx=fidx: e.dma_start(
                hsp_d[fidx * 128:(fidx + 1) * 128, :],
                hst[:, (fidx % 2) * 1024:(fidx % 2 + 1) * 1024]),
                waits=[("sF1", p.val("sF1"))], inc="dHsp%d" % (fidx % 2), inc_amt=16)
    tF1_all = p.val("tF1")
    dHsp_tot = ("dHsp0", p.val("dHsp0")), ("dHsp1", p.val("dHsp1"))

    if PHASES < 4:
        return _finish(p, nc)
    out2 = A32
    stats_first = [True]
    for half in range(2):
        for g in range(8):
            gslot = g % 2
            fbase = g * 16
            gi = half * 8 + g
            hdst = arB[:, gslot * 16384:(gslot + 1) * 16384].rearrange("p (k t) -> p k t", k=16)
            hsrc = hsp_d.rearrange("(k p) t -> p k t", p=128)[:, fbase:fbase + 16, :]
            sem_hg = "dHg%d" % gslot
            hw = [("tF1", tF1_all), dHsp_tot[0], dHsp_tot[1]]
            if gi >= 2:
                hw.append(("tF2", (gi - 1) * 32))
            p.emit("sy", lambda e, hdst=hdst, hsrc=hsrc: e.dma_start(hdst, hsrc),
                   waits=hw, inc=sem_hg, inc_amt=16)
            hg_val = p.val(sem_hg)
            for db in range(4):
                dblk = half * 2048 + db * 512
                wdst = arC[:, gslot * 8192:(gslot + 1) * 8192].rearrange("p (k d) -> p k d", k=16)
                wsrc = w_out.rearrange("(k p) d -> p k d", p=128)[:, fbase:fbase + 16, dblk:dblk + 512]
                sem_wo = "dWo%d" % gslot
                ww = [("tF1", tF1_all)]
                wi = gi * 4 + db
                if wi >= 1:
                    ww.append(("tF2", wi * 8))
                p.emit("gp", lambda e, wdst=wdst, wsrc=wsrc: e.dma_start(wdst, wsrc),
                       waits=ww, inc=sem_wo, inc_amt=16)
                wo_val = p.val(sem_wo)
                for dt in range(4):
                    dti = db * 4 + dt
                    for T in range(2):
                        bank = PS[(dti * 2 + T) % 4]
                        mw = [(sem_hg, hg_val), (sem_wo, wo_val)]
                        cnt = p.val("tF2")
                        if cnt >= 4:
                            mw.append(("vF2", cnt - 3))

                        def mk(e, gslot=gslot, dt=dt, T=T, bank=bank):
                            hb = arB[:, gslot * 16384:(gslot + 1) * 16384].rearrange(
                                "p (k t) -> p k t", k=16)
                            wb = arC[:, gslot * 8192:(gslot + 1) * 8192].rearrange(
                                "p (k d) -> p k d", k=16)
                            for k in range(16):
                                ins = e.matmul(bank[:],
                                               wb[:, k, dt * 128:(dt + 1) * 128],
                                               hb[:, k, T * 512:(T + 1) * 512],
                                               start=(k == 0), stop=(k == 15))
                            return ins
                        p.emit("te", mk, waits=mw, inc="tF2")
                        o_ap = out2[:, dti * 1024 + T * 512: dti * 1024 + T * 512 + 512]
                        if g == 0:
                            p.emit("ve", lambda e, o_ap=o_ap, bank=bank: e.tensor_copy(o_ap, bank[:]),
                                   waits=[("tF2", p.val("tF2"))], inc="vF2")
                        else:
                            p.emit("ve", lambda e, o_ap=o_ap, bank=bank: e.tensor_tensor(
                                o_ap, o_ap, bank[:], op=ALU.add),
                                waits=[("tF2", p.val("tF2"))], inc="vF2")
        for dti in range(16 if SUB != 'a' else 0):
            dglob = half * 16 + dti
            xslot = dglob % 2
            sem_xb = "dXb%d" % xslot
            xw = [dXf_tot[0], dXf_tot[1], ("vFin", p.val("vFin"))]
            p.emit("sy", lambda e, dglob=dglob, xslot=xslot: e.dma_start(
                xfbuf[:, xslot * 1024:(xslot + 1) * 1024],
                xf_d[dglob * 128:(dglob + 1) * 128, :]),
                waits=xw, inc=sem_xb, inc_amt=16)
            xb_val = p.val(sem_xb)
            ya = y1t[:]
            vw = [("vF2", p.val("vF2")), (sem_xb, xb_val),
                  ("dY1", 16 * dglob), ("tSt", p.val("tSt"))]
            p.emit("ve", lambda e, ya=ya, dti=dti, dglob=dglob: e.tensor_scalar(
                ya, out2[:, dti * 1024:(dti + 1) * 1024], 0.5,
                bhalf[:, dglob:dglob + 1], op0=ALU.mult, op1=ALU.add),
                waits=vw + [("vC", vC_all)], inc="vFin")
            p.emit("ve", lambda e, ya=ya, xslot=xslot: e.tensor_tensor(
                ya, ya, xfbuf[:, xslot * 1024:(xslot + 1) * 1024], op=ALU.add),
                inc="vFin")
            vfin = p.val("vFin")
            p.emit("sy", lambda e, ya=ya, dglob=dglob: e.dma_start(
                y1_out[dglob * 128:(dglob + 1) * 128, :], ya),
                waits=[("vFin", vfin)], inc="dY1", inc_amt=16)
            if SUB == 'b':
                continue
            p.emit("se", lambda e, ya=ya: e.activation(y1sq[:], ya, AF.Square),
                   waits=[("vFin", vfin), ("tSt", p.val("tSt"))], inc="sSq")
            first = stats_first[0]
            stats_first[0] = False
            last = (dglob == 31)
            for T in range(2):
                p.emit("te", lambda e, ya=ya, T=T, first=first, last=last: e.matmul(
                    PST[T][:], ones[:, 0:1], ya[:, T * 512:(T + 1) * 512],
                    start=first, stop=last, skip_group_check=True),
                    waits=[("vFin", vfin), ("vC", vC_all)], inc="tSt")
                p.emit("te", lambda e, T=T, first=first, last=last: e.matmul(
                    PST[2 + T][:], ones[:, 0:1], y1sq[:, T * 512:(T + 1) * 512],
                    start=first, stop=last, skip_group_check=True),
                    waits=[("sSq", p.val("sSq"))], inc="tSt")
    dY1_all = p.val("dY1")
    if SUB in ('a', 'b'):
        return _finish(p, nc)
    tSt_all = p.val("tSt")
    tF2_all = p.val("tF2")
    vFin_all = p.val("vFin")
    if SUB == 's':
        return _finish(p, nc)

    for T in range(2):
        p.emit("ve", lambda e, T=T: e.tensor_copy(RA[:, T * 512:(T + 1) * 512], PST[T][:]),
               waits=[("tSt", tSt_all), ("vFin", vFin_all)], inc="vLn")
        p.emit("ve", lambda e, T=T: e.tensor_copy(RB[:, T * 512:(T + 1) * 512], PST[2 + T][:]),
               waits=[("tSt", tSt_all)], inc="vLn")
    p.emit("ve", lambda e: e.tensor_scalar(RA, RA, 1.0 / D, None, op0=ALU.mult),
           waits=[("vLn", p.val("vLn"))], inc="vLn")
    vmean = p.val("vLn")
    p.emit("sy", lambda e: e.dma_start(rows_d[0:1, :], RA),
           waits=[("vLn", vmean)], inc="dBc0", inc_amt=16)
    p.emit("sy", lambda e: e.dma_start(mb[:], rows_d[0:1, :].broadcast_to([128, TPC])),
           waits=[("dBc0", 16)], inc="dBc0", inc_amt=16)
    p.emit("ve", lambda e: e.tensor_tensor(RA, RA, RA, op=ALU.mult),
           waits=[("dBc0", 16)], inc="vLn")
    p.emit("ve", lambda e: e.tensor_scalar(RB, RB, 1.0 / D, LN_EPS, op0=ALU.mult, op1=ALU.add),
           inc="vLn")
    p.emit("ve", lambda e: e.tensor_tensor(RB, RB, RA, op=ALU.subtract), inc="vLn")
    vln = p.val("vLn")
    if SUB == 'c2':
        return _finish(p, nc)
    p.emit("se", lambda e: e.activation(RB, RB, AF.Ln),
           waits=[("vLn", vln)], inc="sLn")
    p.emit("ve", lambda e: e.tensor_scalar(RB, RB, -0.5, None, op0=ALU.mult),
           waits=[("sLn", p.val("sLn"))], inc="vLn")
    p.emit("se", lambda e: e.activation(RB, RB, AF.Exp),
           waits=[("vLn", p.val("vLn"))], inc="sLn")
    sln = p.val("sLn")
    p.emit("sy", lambda e: e.dma_start(rows_d[1:2, :], RB),
           waits=[("sLn", sln)], inc="dBc1", inc_amt=16)
    p.emit("sy", lambda e: e.dma_start(rb[:], rows_d[1:2, :].broadcast_to([128, TPC])),
           waits=[("dBc1", 16)], inc="dBc1", inc_amt=16)
    vln2 = p.val("vLn")

    if PHASES < 5:
        return _finish(p, nc)
    xn2 = arB[:]
    for k in range(32):
        slot = k % 2
        xw = [("dY1", dY1_all), ("vX2", p.val("vX2"))]
        p.emit("sy", lambda e, k=k, slot=slot: e.dma_start(
            xfbuf[:, slot * 1024:(slot + 1) * 1024], y1_out[k * 128:(k + 1) * 128, :]),
            waits=xw, inc="dY1r", inc_amt=16)
        p.emit("ve", lambda e, k=k, slot=slot: e.tensor_tensor(
            y1sq[:], xfbuf[:, slot * 1024:(slot + 1) * 1024], mb[:], op=ALU.subtract),
            waits=[("dY1r", 16 * (k + 1)), ("dBc0", 32), ("dBc1", 32),
                   ("tF2", tF2_all), ("sSq", p.val("sSq"))], inc="vX2")
        p.emit("ve", lambda e, k=k: e.tensor_tensor(
            xn2[:, k * 1024:(k + 1) * 1024], y1sq[:], rb[:], op=ALU.mult),
            inc="vX2")
    vX2_all = p.val("vX2")

    if PHASES < 6:
        return _finish(p, nc)
    wkva = arC[:, 0:10240].rearrange("p (k o) -> p k o", k=32)
    p.emit("gp", lambda e: e.dma_start(wkva, wkvaT_in.rearrange("(k p) o -> p k o", p=128)),
           waits=[("tF2", tF2_all)], inc="dWp1", inc_amt=16)
    for (o0, ow, dstT) in ((0, 128, ck0), (128, 128, ck1), (256, 64, kpe)):
        for T in range(2):
            bank = PS[2 + T]

            def mk(e, o0=o0, ow=ow, T=T, bank=bank):
                for k in range(32):
                    ins = e.matmul(bank[0:ow, :],
                                   wkva[:, k, o0:o0 + ow],
                                   xn2[:, k * 1024 + T * 512: k * 1024 + T * 512 + 512],
                                   start=(k == 0), stop=(k == 31))
                return ins
            p.emit("te", mk, waits=[("dWp1", 16), ("vX2", vX2_all), ("vP", p.val("vP"))], inc="tP")
            p.emit("ve", lambda e, dstT=dstT, ow=ow, T=T, bank=bank: e.tensor_copy(
                dstT[0:ow, T * 512:(T + 1) * 512], bank[0:ow, :]),
                waits=[("tP", p.val("tP"))], inc="vP")
    vP_ck = p.val("vP")

    p.emit("se", lambda e: e.activation(y1t[:], ck0[:], AF.Square),
           waits=[("vP", vP_ck), ("dY1", dY1_all)], inc="sP")
    sP_sq0 = p.val("sP")
    for T in range(2):
        p.emit("te", lambda e, T=T: e.matmul(PST[T][:], ones[:, 0:1],
                                             y1t[:, T * 512:T * 512 + 512],
                                             start=True, stop=False, skip_group_check=True),
               waits=[("sP", sP_sq0), ("vLn", vln2)], inc="tP")
    p.emit("se", lambda e: e.activation(y1t[:], ck1, AF.Square),
           waits=[("tP", p.val("tP"))], inc="sP")
    sP_sq1 = p.val("sP")
    for T in range(2):
        p.emit("te", lambda e, T=T: e.matmul(PST[T][:], ones[:, 0:1],
                                             y1t[:, T * 512:T * 512 + 512],
                                             start=False, stop=True, skip_group_check=True),
               waits=[("sP", sP_sq1)], inc="tP")
    for T in range(2):
        p.emit("ve", lambda e, T=T: e.tensor_copy(RA[:, T * 512:(T + 1) * 512], PST[T][:]),
               waits=[("tP", p.val("tP"))], inc="vP")
    p.emit("ve", lambda e: e.tensor_scalar(RA, RA, 1.0 / KVR, RMS_EPS, op0=ALU.mult, op1=ALU.add),
           waits=[("vP", p.val("vP"))], inc="vP")
    p.emit("se", lambda e: e.activation(RA, RA, AF.Ln),
           waits=[("vP", p.val("vP"))], inc="sP")
    p.emit("ve", lambda e: e.tensor_scalar(RA, RA, -0.5, None, op0=ALU.mult),
           waits=[("sP", p.val("sP"))], inc="vP")
    p.emit("se", lambda e: e.activation(RA, RA, AF.Exp),
           waits=[("vP", p.val("vP"))], inc="sP")
    sP_r = p.val("sP")
    p.emit("sy", lambda e: e.dma_start(rows_d[2:3, :], RA),
           waits=[("sP", sP_r)], inc="dBc2", inc_amt=16)
    p.emit("sy", lambda e: e.dma_start(rb[:], rows_d[2:3, :].broadcast_to([128, TPC])),
           waits=[("dBc2", 16), ("vX2", vX2_all)], inc="dBc2", inc_amt=16)
    p.emit("ve", lambda e: e.tensor_tensor(ck0[:], ck0[:], rb[:], op=ALU.mult),
           waits=[("dBc2", 32)], inc="vP")
    p.emit("ve", lambda e: e.tensor_tensor(ck1, ck1, rb[:], op=ALU.mult), inc="vP")
    vP_ckn = p.val("vP")
    p.emit("sy", lambda e: e.dma_start(con_out[0:128, :], ck0[:]),
           waits=[("vP", vP_ckn)], inc="dCon", inc_amt=16)
    p.emit("sy", lambda e: e.dma_start(con_out[128:256, :], ck1),
           waits=[("vP", vP_ckn)], inc="dCon", inc_amt=16)

    p.emit("ve", lambda e: e.tensor_tensor(rop1, kpe[0:32, :], cosT[0:32, :], op=ALU.mult),
           waits=[("vP", vP_ckn)], inc="vP")
    p.emit("ve", lambda e: e.tensor_tensor(rop2, kpe[32:64, :], sinT[32:64, :], op=ALU.mult),
           inc="vP")
    p.emit("ve", lambda e: e.tensor_tensor(kpr[0:32, :], rop1, rop2, op=ALU.subtract), inc="vP")
    p.emit("ve", lambda e: e.tensor_tensor(rop1, kpe[32:64, :], cosT[32:64, :], op=ALU.mult),
           inc="vP")
    p.emit("ve", lambda e: e.tensor_tensor(rop2, kpe[0:32, :], sinT[0:32, :], op=ALU.mult),
           inc="vP")
    p.emit("ve", lambda e: e.tensor_tensor(kpr[32:64, :], rop1, rop2, op=ALU.add), inc="vP")
    p.emit("sy", lambda e: e.dma_start(con_out[256:320, :], kpr),
           waits=[("vP", p.val("vP"))], inc="dCon", inc_amt=16)

    wqa = arA[:].rearrange("p (k o) -> p k o", k=32)
    p.emit("gp", lambda e: e.dma_start(wqa, wqaT_in.rearrange("(k p) o -> p k o", p=128)),
           waits=[("vFin", vFin_all), ("tSt", tSt_all)], inc="dWp2", inc_amt=16)
    qmid = arC[:, 0:8192]
    for qt in range(8):
        for T in range(2):
            bank = PS[2 + T]

            def mk(e, qt=qt, T=T, bank=bank):
                for k in range(32):
                    ins = e.matmul(bank[:],
                                   wqa[:, k, qt * 128:(qt + 1) * 128],
                                   xn2[:, k * 1024 + T * 512: k * 1024 + T * 512 + 512],
                                   start=(k == 0), stop=(k == 31))
                return ins
            p.emit("te", mk, waits=[("dWp2", 16), ("vX2", vX2_all), ("vP", p.val("vP"))], inc="tP")
            p.emit("ve", lambda e, qt=qt, T=T, bank=bank: e.tensor_copy(
                qmid[:, qt * 1024 + T * 512: qt * 1024 + T * 512 + 512], bank[:]),
                waits=[("tP", p.val("tP"))], inc="vP")
    vP_qm = p.val("vP")
    for qt in range(8):
        p.emit("se", lambda e, qt=qt: e.activation(y1sq[:], qmid[:, qt * 1024:(qt + 1) * 1024],
                                                   AF.Square),
               waits=[("vP", vP_qm), ("tQs", p.val("tQs")), ("dCon", 48)], inc="sP")
        for T in range(2):
            p.emit("te", lambda e, qt=qt, T=T: e.matmul(
                PST[2 + T][:], ones[:, 0:1], y1sq[:, T * 512:(T + 1) * 512],
                start=(qt == 0), stop=(qt == 7), skip_group_check=True),
                waits=[("sP", p.val("sP"))], inc="tQs")
    tQs_all = p.val("tQs")
    for T in range(2):
        p.emit("ve", lambda e, T=T: e.tensor_copy(RA[:, T * 512:(T + 1) * 512], PST[2 + T][:]),
               waits=[("tQs", tQs_all)], inc="vP")
    p.emit("ve", lambda e: e.tensor_scalar(RA, RA, 1.0 / 1024.0, RMS_EPS, op0=ALU.mult, op1=ALU.add),
           waits=[("vP", p.val("vP"))], inc="vP")
    p.emit("se", lambda e: e.activation(RA, RA, AF.Ln),
           waits=[("vP", p.val("vP"))], inc="sP")
    p.emit("ve", lambda e: e.tensor_scalar(RA, RA, -0.5, None, op0=ALU.mult),
           waits=[("sP", p.val("sP"))], inc="vP")
    p.emit("se", lambda e: e.activation(RA, RA, AF.Exp),
           waits=[("vP", p.val("vP"))], inc="sP")
    sP_qr = p.val("sP")
    p.emit("sy", lambda e: e.dma_start(rows_d[3:4, :], RA),
           waits=[("sP", sP_qr)], inc="dBc3", inc_amt=16)
    p.emit("sy", lambda e: e.dma_start(rb[:], rows_d[3:4, :].broadcast_to([128, TPC])),
           waits=[("dBc3", 16), ("vP", p.val("vP"))], inc="dBc3", inc_amt=16)
    vP_qrb = p.val("vP")
    qmidn = arC[:, 8192:16384].rearrange("p (k t) -> p k t", k=8)
    for qt in range(8):
        p.emit("ve", lambda e, qt=qt: e.tensor_tensor(
            qmidn[:, qt, :], qmid[:, qt * 1024:(qt + 1) * 1024], rb[:], op=ALU.mult),
            waits=[("vP", vP_qrb), ("dBc3", 32)], inc="vP")
    vP_qmn = p.val("vP")

    wqb = arA[:].rearrange("p (k o) -> p k o", k=8)
    p.emit("gp", lambda e: e.dma_start(wqb, wqbT_in.rearrange("(k p) o -> p k o", p=128)),
           waits=[("tQs", tQs_all), ("tP", p.val("tP"))], inc="dWp3", inc_amt=16)
    dWp_qb = p.val("dWp3")
    for k in range(8):
        p.emit("ve", lambda e, k=k: e.tensor_scalar(wqb[:, k, :], wqb[:, k, :],
                                                    wlnqs[:, k:k + 1], None, op0=ALU.mult),
               waits=[("dWp3", dWp_qb), ("vC", vC_all)], inc="vQb")
    vQb_all = p.val("vQb")
    for qt in range(32):
        for T in range(2):
            bank = PS[2 + T]

            def mk(e, qt=qt, T=T, bank=bank):
                for k in range(8):
                    ins = e.matmul(bank[:],
                                   wqb[:, k, qt * 128:(qt + 1) * 128],
                                   qmidn[:, k, T * 512:(T + 1) * 512],
                                   start=(k == 0), stop=(k == 7))
                return ins
            p.emit("te", mk, waits=[("vQb", vQb_all), ("vP", vP_qmn), ("vQ", p.val("vQ"))],
                   inc="tQ")
            p.emit("ve", lambda e, T=T, bank=bank: e.tensor_copy(
                y1t[:, T * 512:(T + 1) * 512], bank[:]),
                waits=[("tQ", p.val("tQ"))], inc="vQ")
        qslot = qt % 2
        sem_q = "dQf%d" % qslot
        qs = qstage[:, qslot * 1024:(qslot + 1) * 1024]
        qw = [("vQ", p.val("vQ")), (sem_q, p.val(sem_q))]
        p.emit("ve", lambda e: e.tensor_tensor(rop1, y1t[0:32, :], cosT[0:32, :],
                                               op=ALU.mult), waits=qw, inc="vQ")
        p.emit("ve", lambda e: e.tensor_tensor(rop2, y1t[32:64, :], sinT[32:64, :],
                                               op=ALU.mult), inc="vQ")
        p.emit("ve", lambda e, qs=qs: e.tensor_tensor(qs[0:32, :], rop1, rop2, op=ALU.subtract),
               inc="vQ")
        p.emit("ve", lambda e: e.tensor_tensor(rop1, y1t[32:64, :], cosT[32:64, :],
                                               op=ALU.mult), inc="vQ")
        p.emit("ve", lambda e: e.tensor_tensor(rop2, y1t[0:32, :], sinT[0:32, :],
                                               op=ALU.mult), inc="vQ")
        p.emit("ve", lambda e, qs=qs: e.tensor_tensor(qs[32:64, :], rop1, rop2, op=ALU.add),
               inc="vQ")
        p.emit("ve", lambda e, qs=qs: e.tensor_copy(qs[64:128, :], y1t[64:128, :]),
               inc="vQ")
        p.emit("sy", lambda e, qt=qt, qs=qs: e.dma_start(qf_out[qt * 128:(qt + 1) * 128, :], qs),
               waits=[("vQ", p.val("vQ"))], inc=sem_q, inc_amt=16)

    p.wait("sy", "dY1", dY1_all)
    p.wait("sy", "dCon", p.val("dCon"))
    p.wait("sy", "dQf0", p.val("dQf0"))
    p.wait("sy", "dQf1", p.val("dQf1"))

    if DBG:
        xnt_o = nc.dram_tensor("xnt_o", [TPC, D], BF16, kind="ExternalOutput").ap()
        xf_o = nc.dram_tensor("xf_o", [D, TPC], F32, kind="ExternalOutput").ap()
        h_o = nc.dram_tensor("h_o", [FF, TPC], BF16, kind="ExternalOutput").ap()
        p.emit("sy", lambda e: e.dma_start(xnt_o, xnt_d[:, :]), inc="dDbg", inc_amt=16)
        p.emit("sy", lambda e: e.dma_start(xf_o, xf_d[:, :]), inc="dDbg", inc_amt=16)
        p.emit("sy", lambda e: e.dma_start(h_o, hsp_d[:, :]), inc="dDbg", inc_amt=16)
        stat_o = nc.dram_tensor("stat_o", [128, 32], F32, kind="ExternalOutput").ap()
        p.emit("sy", lambda e: e.dma_start(stat_o, stat[:]), inc="dDbg", inc_amt=16)
        p.wait("sy", "dDbg", p.val("dDbg"))

    p.run()
    return nc


NSLOT = 18
KTOT = NSLOT * 512


def build_l2():
    nc = bass.Bass("TRN2", target_bir_lowering=False, detect_race_conditions=False)
    p = Prog(nc)

    y1_in = nc.dram_tensor("y1F", [D, TPC], F32, kind="ExternalInput").ap()
    qf_in = nc.dram_tensor("qF", [D, TPC], BF16, kind="ExternalInput").ap()
    g_in = nc.dram_tensor("G", [320, S], F32, kind="ExternalInput").ap()
    gown_in = nc.dram_tensor("Gown", [320, TPC], F32, kind="ExternalInput").ap()
    vis_in = nc.dram_tensor("visb", [128, 26], F32, kind="ExternalInput").ap()
    tri_in = nc.dram_tensor("triT", [512, 512], F32, kind="ExternalInput").ap()
    wkvbT_in = nc.dram_tensor("wkvbT", [KVR, 6144], F32, kind="ExternalInput").ap()
    wlnkv_in = nc.dram_tensor("wlnkv", [128, 2], F32, kind="ExternalInput").ap()
    woT_in = nc.dram_tensor("woT", [D, D], F32, kind="ExternalInput").ap()
    w_in = nc.dram_tensor("w_in", [D, FF], F32, kind="ExternalInput").ap()
    binT_in = nc.dram_tensor("binT", [128, 128], F32, kind="ExternalInput").ap()
    w_out = nc.dram_tensor("w_out", [FF, D], F32, kind="ExternalInput").ap()
    boutT_in = nc.dram_tensor("boutT", [128, 32], F32, kind="ExternalInput").ap()

    y3_out = nc.dram_tensor("y3F", [D, TPC], F32, kind="ExternalOutput").ap()

    y2_d = nc.dram_tensor("y2_d", [D, TPC], F32, kind="Internal").ap()
    hsp_d = nc.dram_tensor("hsp_d", [FF, TPC], BF16, kind="Internal").ap()
    rows_d = nc.dram_tensor("rows_d", [4, TPC], F32, kind="Internal").ap()

    arA = nc.alloc_sbuf_tensor("arA", [128, 32768], BF16)
    arB = nc.alloc_sbuf_tensor("arB", [128, 32768], BF16)
    arC = nc.alloc_sbuf_tensor("arC", [128, 16384], BF16)
    vTs = nc.alloc_sbuf_tensor("vTs", [128, 9216], BF16)
    A32 = arA[:].bitcast(F32)

    triS = nc.alloc_sbuf_tensor("triSS", [128, 2048], F32)
    y1t = triS[:, 0:1024]
    y1sq = triS[:, 1024:2048]
    AUX = nc.alloc_sbuf_tensor("auxS", [1, 2048], F32)
    RA = AUX[0:1, 0:1024]
    RB = AUX[0:1, 1024:2048]
    mbrb = vTs[:].bitcast(F32)
    mb = mbrb[:, 0:1024]
    rb = mbrb[:, 1024:2048]
    xfbuf = nc.alloc_sbuf_tensor("xfbufS", [128, 2048], F32)
    rbc = nc.alloc_sbuf_tensor("rbcS", [128, 1024], F32)
    binT = nc.alloc_sbuf_tensor("binTS", [128, 128], F32)
    bhalf = nc.alloc_sbuf_tensor("bhalfS", [128, 32], F32)
    visb = nc.alloc_sbuf_tensor("visbS", [128, 26], F32)
    wlnkv = nc.alloc_sbuf_tensor("wlnkvS", [128, 2], F32)
    onesb = nc.alloc_sbuf_tensor("onesbS", [128, 8], BF16)
    onesf = nc.alloc_sbuf_tensor("onesfS", [128, 8], F32)

    ckvG0 = arB[:, 0:9216]
    ckvG1 = arB[:, 9216:18432]
    k_full = arB[:, 18432:27648]
    qh_slots = [arB[:, 27648 + i * 1024: 27648 + (i + 1) * 1024] for i in range(2)]
    prb_slots = [arB[:, 29696 + i * 512: 29696 + (i + 1) * 512] for i in range(2)]
    wkvb0 = arC[:, 0:6144]
    wkvb1 = arC[:, 6144:12288]

    PS = [nc.alloc_psum_tensor(f"ps{i}", [128, 512], F32) for i in range(4)]
    PST = [nc.alloc_psum_tensor(f"pst{i}", [1, 512], F32) for i in range(4)]

    p.emit("sy", lambda e: e.dma_start(binT[:], binT_in), inc="dC", inc_amt=16)
    p.emit("sy", lambda e: e.dma_start(bhalf[:], boutT_in), inc="dC", inc_amt=16)
    p.emit("sy", lambda e: e.dma_start(visb[:], vis_in), inc="dC", inc_amt=16)
    p.emit("sy", lambda e: e.dma_start(wlnkv[:], wlnkv_in), inc="dC", inc_amt=16)
    p.emit("sy", lambda e: e.dma_start(
        triS[:].rearrange("p (kk q) -> p kk q", kk=4),
        tri_in.rearrange("(kk p) q -> p kk q", p=128)), inc="dC", inc_amt=16)
    dC_all = p.val("dC")
    p.emit("ve", lambda e: e.tensor_scalar(bhalf[:], bhalf[:], 0.5, None, op0=ALU.mult),
           waits=[("dC", dC_all)], inc="vC")
    p.emit("ve", lambda e: e.memset(onesb[:], 1.0), inc="vC")
    p.emit("ve", lambda e: e.memset(onesf[:], 1.0), inc="vC")

    def slot_src(sl):
        if sl == 0:
            return gown_in[:, 0:512]
        if sl == 1:
            return gown_in[:, 512:1024]
        j = sl - 2
        return g_in[:, j * 512:(j + 1) * 512]
    for sl in range(NSLOT):
        srcs = slot_src(sl)
        p.emit("gp", lambda e, sl=sl, srcs=srcs: e.dma_start(
            ckvG0[:, sl * 512:(sl + 1) * 512], srcs[0:128, :]), inc="dKg", inc_amt=16)
        p.emit("gp", lambda e, sl=sl, srcs=srcs: e.dma_start(
            ckvG1[:, sl * 512:(sl + 1) * 512], srcs[128:256, :]), inc="dKg", inc_amt=16)
        p.emit("gp", lambda e, sl=sl, srcs=srcs: e.dma_start(
            k_full[0:64, sl * 512:(sl + 1) * 512], srcs[256:320, :]), inc="dKg", inc_amt=16)
    dKg_all = p.val("dKg")
    p.emit("gp", lambda e: e.dma_start(wkvb0[:], wkvbT_in[0:128, :]), inc="dKw", inc_amt=16)
    p.emit("gp", lambda e: e.dma_start(wkvb1[:], wkvbT_in[128:256, :]), inc="dKw", inc_amt=16)
    dKw_all = p.val("dKw")
    p.emit("ve", lambda e: e.tensor_scalar(wkvb0[:], wkvb0[:], wlnkv[:, 0:1], None, op0=ALU.mult),
           waits=[("dKw", dKw_all), ("dC", dC_all)], inc="vC")
    p.emit("ve", lambda e: e.tensor_scalar(wkvb1[:], wkvb1[:], wlnkv[:, 1:2], None, op0=ALU.mult),
           inc="vC")
    vC_all = p.val("vC")

    seglists = [[0] + [2 + j for j in range(8)],
                [1] + [2 + j for j in range(16)]]
    rbi = [0]
    last_tt = {0: 0, 1: 0}
    bank_free = {}
    prb_free = {}
    out_done = [0]
    recip_done = [0]
    ra_free = [0]
    ra_store = [None]

    for h in range(32):
        qslot = h % 2
        sem_qh = "dQh%d" % qslot
        qw = [("tA", p.val("tA"))] if h >= 2 else []
        p.emit("sy", lambda e, h=h, qslot=qslot: e.dma_start(
            qh_slots[qslot], qf_in[h * 128:(h + 1) * 128, :]),
            waits=qw, inc=sem_qh, inc_amt=16)
        qh_val = p.val(sem_qh)
        for sl in range(NSLOT):
            kw = [("dKg", dKg_all), ("vC", vC_all)]
            if h > 0 or sl > 0:
                kw.append(("sA", p.val("sA")))
            def mkk(e, h=h, sl=sl):
                e.matmul(PS[1][0:64, :], wkvb0[:, h * 192:h * 192 + 64],
                         ckvG0[:, sl * 512:(sl + 1) * 512], start=True, stop=False)
                return e.matmul(PS[1][0:64, :], wkvb1[:, h * 192:h * 192 + 64],
                                ckvG1[:, sl * 512:(sl + 1) * 512], start=False, stop=True)
            p.emit("te", mkk, waits=kw, inc="tA")
            p.emit("se", lambda e, sl=sl: e.activation(
                k_full[64:128, sl * 512:(sl + 1) * 512], PS[1][0:64, :], AF.Identity),
                waits=[("tA", p.val("tA"))], inc="sA")
        for j in range(72):
            def mkv(e, h=h, j=j):
                e.matmul(PS[1][:, 0:128], ckvG0[:, j * 128:(j + 1) * 128],
                         wkvb0[:, h * 192 + 64:h * 192 + 192], start=True, stop=False)
                return e.matmul(PS[1][:, 0:128], ckvG1[:, j * 128:(j + 1) * 128],
                                wkvb1[:, h * 192 + 64:h * 192 + 192], start=False, stop=True)
            p.emit("te", mkv, waits=[("sA", p.val("sA"))], inc="tA")
            p.emit("se", lambda e, j=j: e.activation(
                vTs[:, j * 128:(j + 1) * 128], PS[1][:, 0:128], AF.Identity),
                waits=[("tA", p.val("tA")), ("tA2", p.val("tA2"))], inc="sA")
        sA_kv = p.val("sA")
        for T in range(2):
            segs = seglists[T]
            nkk = len(segs) * 4
            ikk = 0
            for si, seg in enumerate(segs):
                for kk4 in range(4):
                    kkg = seg * 4 + kk4
                    bank = PS[2 + (ikk % 2)]
                    mw = [("sA", sA_kv), (sem_qh, qh_val),
                          ("sA", bank_free.get(2 + (ikk % 2), 0))]
                    def mks(e, kkg=kkg, T=T, bank=bank, h=h):
                        return e.matmul(bank[:], k_full[:, kkg * 128:(kkg + 1) * 128],
                                        qh_slots[h % 2][:, T * 512:(T + 1) * 512],
                                        start=True, stop=True)
                    p.emit("te", mks, waits=mw, inc="tA")
                    if si == 0:
                        p.emit("ve", lambda e, kk4=kk4, bank=bank: e.tensor_tensor(
                            bank[:], bank[:], triS[:, kk4 * 512:(kk4 + 1) * 512], op=ALU.add),
                            waits=[("tA", p.val("tA"))], inc="vA")
                    col = si if T == 0 else 9 + si
                    prslot = ikk % 2
                    ew = [("tA", p.val("tA")), ("vA", p.val("vA")),
                          ("tA2", prb_free.get(prslot, 0))]
                    p.emit("se", lambda e, col=col, prslot=prslot, bank=bank: e.activation(
                        prb_slots[prslot], bank[:], AF.Exp, bias=visb[:, col:col + 1]),
                        waits=ew, inc="sA")
                    bank_free[2 + (ikk % 2)] = p.val("sA")
                    sA_e = p.val("sA")
                    vw0 = [("sA", sA_e)]
                    if ikk == 0:
                        vw0.append(("vA", out_done[0]))
                    p.emit("te", lambda e, kkg=kkg, prslot=prslot, first=(ikk == 0), last=(ikk == nkk - 1): e.matmul(
                        PS[0][:], vTs[:, kkg * 128:(kkg + 1) * 128], prb_slots[prslot],
                        start=first, stop=last, skip_group_check=True),
                        waits=vw0, inc="tA2")
                    vw1 = []
                    if ikk == 0:
                        vw1.append(("vA", recip_done[0]))
                    p.emit("te", lambda e, prslot=prslot, first=(ikk == 0), last=(ikk == nkk - 1): e.matmul(
                        PST[0][:], onesb[:, 0:1], prb_slots[prslot],
                        start=first, stop=last, skip_group_check=True),
                        waits=vw1, inc="tA2")
                    prb_free[prslot] = p.val("tA2")
                    ikk += 1
            tA2_done = p.val("tA2")
            par = rbi[0] % 2
            sem_b = "dBq%d" % par
            rw = [("tA2", tA2_done)]
            if ra_store[0] is not None:
                rw.append(ra_store[0])
            p.emit("ve", lambda e: e.reciprocal(RA[:, 0:512], PST[0][:]),
                   waits=rw, inc="vA")
            recip_done[0] = p.val("vA")
            vA_r = p.val("vA")
            bw = [("vA", vA_r), ("vA", last_tt[par])]
            p.emit("sy", lambda e, par=par: e.dma_start(rows_d[2 + par:3 + par, 0:512], RA[:, 0:512]),
                   waits=bw, inc=sem_b, inc_amt=16)
            ra_free_sem = (sem_b, p.val(sem_b))
            p.emit("sy", lambda e, par=par: e.dma_start(
                rbc[:, par * 512:(par + 1) * 512],
                rows_d[2 + par:3 + par, 0:512].broadcast_to([128, 512])),
                waits=[(sem_b, p.val(sem_b))], inc=sem_b, inc_amt=16)
            dB_done = p.val(sem_b)
            p.emit("ve", lambda e, h=h, T=T, par=par: e.tensor_tensor(
                arA[:, h * 1024 + T * 512: h * 1024 + (T + 1) * 512],
                PS[0][:], rbc[:, par * 512:(par + 1) * 512], op=ALU.mult),
                waits=[(sem_b, dB_done), ("tA2", tA2_done)], inc="vA")
            out_done[0] = p.val("vA")
            last_tt[par] = p.val("vA")
            ra_store[0] = ra_free_sem
            rbi[0] += 1
    tA_all = p.val("tA")
    tA2_all = p.val("tA2")
    vA_all = p.val("vA")
    sA_all = p.val("sA")

    stats_first = [True]
    for dt in range(32):
        wslot = dt % 2
        sem_ww = "dWw%d" % wslot
        wow = arC[:, wslot * 4096:(wslot + 1) * 4096].rearrange("p (k d) -> p k d", k=32)
        ww = [("sA", sA_all)]
        if dt >= 2:
            ww.append(("tW", p.val("tW")))
        p.emit("gp", lambda e, wow=wow, dt=dt: e.dma_start(
            wow, woT_in.rearrange("(k p) d -> p k d", p=128)[:, :, dt * 128:(dt + 1) * 128]),
            waits=ww, inc=sem_ww, inc_amt=16)
        ww_val = p.val(sem_ww)
        xslot = dt % 2
        sem_xr = "dXr%d" % xslot
        xw = [("vA", vA_all), ("vW", p.val("vW"))]
        p.emit("sy", lambda e, dt=dt, xslot=xslot: e.dma_start(
            xfbuf[:, xslot * 1024:(xslot + 1) * 1024], y1_in[dt * 128:(dt + 1) * 128, :]),
            waits=xw, inc=sem_xr, inc_amt=16)
        xr_val = p.val(sem_xr)
        for T in range(2):
            bank = PS[2 + T]
            mw = [(sem_ww, ww_val), ("vA", vA_all)]
            if dt >= 1:
                mw.append(("vW", p.val("vW") - 1))
            def mkw(e, wow=wow, T=T, bank=bank):
                for k in range(32):
                    ins = e.matmul(bank[:], wow[:, k, :],
                                   arA[:, k * 1024 + T * 512: k * 1024 + T * 512 + 512],
                                   start=(k == 0), stop=(k == 31))
                return ins
            p.emit("te", mkw, waits=mw, inc="tW")
            p.emit("ve", lambda e, T=T, bank=bank, xslot=xslot: e.tensor_tensor(
                y1t[:, T * 512:(T + 1) * 512],
                bank[:], xfbuf[:, xslot * 1024 + T * 512: xslot * 1024 + T * 512 + 512],
                op=ALU.add),
                waits=[("tW", p.val("tW")), (sem_xr, xr_val), ("dY2", 16 * dt),
                       ("tSt", p.val("tSt"))], inc="vW")
        vW_y = p.val("vW")
        p.emit("sy", lambda e, dt=dt: e.dma_start(y2_d[dt * 128:(dt + 1) * 128, :], y1t),
               waits=[("vW", vW_y)], inc="dY2", inc_amt=16)
        p.emit("se", lambda e: e.activation(y1sq, y1t, AF.Square),
               waits=[("vW", vW_y), ("tSt", p.val("tSt"))], inc="sW")
        first = stats_first[0]
        stats_first[0] = False
        last = (dt == 31)
        for T in range(2):
            p.emit("te", lambda e, T=T, first=first, last=last: e.matmul(
                PST[T][:], onesf[:, 0:1], y1t[:, T * 512:(T + 1) * 512],
                start=first, stop=last, skip_group_check=True),
                waits=[("vW", vW_y)], inc="tSt")
            p.emit("te", lambda e, T=T, first=first, last=last: e.matmul(
                PST[2 + T][:], onesf[:, 0:1], y1sq[:, T * 512:(T + 1) * 512],
                start=first, stop=last, skip_group_check=True),
                waits=[("sW", p.val("sW"))], inc="tSt")
    dY2_all = p.val("dY2")
    tSt_all = p.val("tSt")
    tW_all = p.val("tW")
    vW_all = p.val("vW")

    for T in range(2):
        p.emit("ve", lambda e, T=T: e.tensor_copy(RA[:, T * 512:(T + 1) * 512], PST[T][:]),
               waits=[("tSt", tSt_all), ("vW", vW_all)], inc="vLn")
        p.emit("ve", lambda e, T=T: e.tensor_copy(RB[:, T * 512:(T + 1) * 512], PST[2 + T][:]),
               waits=[("tSt", tSt_all)], inc="vLn")
    p.emit("ve", lambda e: e.tensor_scalar(RA, RA, 1.0 / D, None, op0=ALU.mult), inc="vLn")
    vmean = p.val("vLn")
    p.emit("sy", lambda e: e.dma_start(rows_d[0:1, :], RA),
           waits=[("vLn", vmean), ("dBq0", p.val("dBq0")), ("dBq1", p.val("dBq1"))],
           inc="dBc0", inc_amt=16)
    p.emit("sy", lambda e: e.dma_start(mb[:], rows_d[0:1, :].broadcast_to([128, TPC])),
           waits=[("dBc0", 16), ("sA", sA_all), ("tA2", tA2_all)], inc="dBc0", inc_amt=16)
    p.emit("ve", lambda e: e.tensor_tensor(RA, RA, RA, op=ALU.mult),
           waits=[("dBc0", 16)], inc="vLn")
    p.emit("ve", lambda e: e.tensor_scalar(RB, RB, 1.0 / D, LN_EPS, op0=ALU.mult, op1=ALU.add),
           inc="vLn")
    p.emit("ve", lambda e: e.tensor_tensor(RB, RB, RA, op=ALU.subtract), inc="vLn")
    p.emit("se", lambda e: e.activation(RB, RB, AF.Ln), waits=[("vLn", p.val("vLn"))], inc="sLn")
    p.emit("ve", lambda e: e.tensor_scalar(RB, RB, -0.5, None, op0=ALU.mult),
           waits=[("sLn", p.val("sLn"))], inc="vLn")
    p.emit("se", lambda e: e.activation(RB, RB, AF.Exp), waits=[("vLn", p.val("vLn"))], inc="sLn")
    p.emit("sy", lambda e: e.dma_start(rows_d[1:2, :], RB),
           waits=[("sLn", p.val("sLn"))], inc="dBc1", inc_amt=16)
    p.emit("sy", lambda e: e.dma_start(rb[:], rows_d[1:2, :].broadcast_to([128, TPC])),
           waits=[("dBc1", 16)], inc="dBc1", inc_amt=16)

    xn3 = arA[:]
    for k in range(32):
        slot = k % 2
        p.emit("sy", lambda e, k=k, slot=slot: e.dma_start(
            xfbuf[:, slot * 1024:(slot + 1) * 1024], y2_d[k * 128:(k + 1) * 128, :]),
            waits=[("dY2", dY2_all), ("vX3", p.val("vX3"))], inc="dY2r", inc_amt=16)
        p.emit("ve", lambda e, k=k, slot=slot: e.tensor_tensor(
            y1sq, xfbuf[:, slot * 1024:(slot + 1) * 1024], mb[:], op=ALU.subtract),
            waits=[("dY2r", 16 * (k + 1)), ("dBc0", 32), ("dBc1", 32), ("tW", tW_all),
                   ("tSt", tSt_all), ("sW", p.val("sW"))], inc="vX3")
        p.emit("ve", lambda e, k=k: e.tensor_tensor(
            xn3[:, k * 1024:(k + 1) * 1024], y1sq, rb[:], op=ALU.mult),
            inc="vX3")
    vX3_all = p.val("vX3")

    winb = arB[:]
    for w in range(32):
        slot = w % 2
        sem_w = "dWin%d" % slot
        dst = winb[:, slot * 16384:(slot + 1) * 16384].rearrange("p (k f) -> p k f", k=32)
        srcw = w_in.rearrange("(k p) f -> p k f", p=128)[:, :, w * 512:(w + 1) * 512]
        waits = [("tA", tA_all), ("tA2", tA2_all), ("sA", sA_all)]
        if w >= 2:
            waits.append(("tF1", 8 * (w - 1)))
        p.emit("gp", lambda e, dst=dst, srcw=srcw: e.dma_start(dst, srcw),
               waits=waits, inc=sem_w, inc_amt=16)
        for j in range(4):
            fidx = w * 4 + j
            for T in range(2):
                bank = PS[(fidx * 2 + T) % 4]
                mwaits = [(sem_w, p.val(sem_w)), ("vX3", vX3_all)]
                if fidx * 2 + T >= 4:
                    mwaits.append(("sF1", fidx * 2 + T - 3))

                def mk(e, slot=slot, j=j, T=T, bank=bank):
                    lb = winb[:, slot * 16384:(slot + 1) * 16384].rearrange(
                        "p (k f) -> p k f", k=32)
                    for k in range(32):
                        ins = e.matmul(bank[:],
                                       lb[:, k, j * 128:(j + 1) * 128],
                                       xn3[:, k * 1024 + T * 512: k * 1024 + T * 512 + 512],
                                       start=(k == 0), stop=(k == 31))
                    return ins
                p.emit("te", mk, waits=mwaits, inc="tF1")
                sem_h = "dHsp%d" % (fidx % 2)
                hw = [("tF1", p.val("tF1")), ("dC", dC_all)]
                if fidx >= 2:
                    hw.append((sem_h, p.val(sem_h)))
                p.emit("se", lambda e, fidx=fidx, T=T, bank=bank: e.activation(
                    xfbuf[:].bitcast(BF16)[:, (fidx % 2) * 2048 + T * 512:
                                           (fidx % 2) * 2048 + T * 512 + 512],
                    bank[:], AF.Gelu_apprx_tanh, bias=binT[:, fidx:fidx + 1]),
                    waits=hw, inc="sF1")
            p.emit("sy", lambda e, fidx=fidx: e.dma_start(
                hsp_d[fidx * 128:(fidx + 1) * 128, :],
                xfbuf[:].bitcast(BF16)[:, (fidx % 2) * 2048:(fidx % 2) * 2048 + 1024]),
                waits=[("sF1", p.val("sF1"))], inc="dHsp%d" % (fidx % 2), inc_amt=16)
    tF1_all = p.val("tF1")
    dHsp_tot = ("dHsp0", p.val("dHsp0")), ("dHsp1", p.val("dHsp1"))

    out2 = A32
    for half in range(2):
        for g in range(8):
            gslot = g % 2
            fbase = g * 16
            gi = half * 8 + g
            sem_hg = "dHg%d" % gslot
            hdst = arB[:, gslot * 16384:(gslot + 1) * 16384].rearrange("p (k t) -> p k t", k=16)
            hsrc = hsp_d.rearrange("(k p) t -> p k t", p=128)[:, fbase:fbase + 16, :]
            hw = [("tF1", tF1_all), dHsp_tot[0], dHsp_tot[1]]
            if gi >= 2:
                hw.append(("tF2", (gi - 1) * 32))
            p.emit("sy", lambda e, hdst=hdst, hsrc=hsrc: e.dma_start(hdst, hsrc),
                   waits=hw, inc=sem_hg, inc_amt=16)
            hg_val = p.val(sem_hg)
            for db in range(4):
                dblk = half * 2048 + db * 512
                sem_wo = "dWo%d" % gslot
                wdst = arC[:, gslot * 8192:(gslot + 1) * 8192].rearrange("p (k d) -> p k d", k=16)
                wsrc = w_out.rearrange("(k p) d -> p k d", p=128)[:, fbase:fbase + 16, dblk:dblk + 512]
                ww2 = [("tF1", tF1_all), ("tW", tW_all)]
                wi = gi * 4 + db
                if wi >= 1:
                    ww2.append(("tF2", wi * 8))
                p.emit("gp", lambda e, wdst=wdst, wsrc=wsrc: e.dma_start(wdst, wsrc),
                       waits=ww2, inc=sem_wo, inc_amt=16)
                wo_val = p.val(sem_wo)
                for dti4 in range(4):
                    dti = db * 4 + dti4
                    for T in range(2):
                        bank = PS[(dti * 2 + T) % 4]
                        mw = [(sem_hg, hg_val), (sem_wo, wo_val)]
                        cnt = p.val("tF2")
                        if cnt >= 4:
                            mw.append(("vF2", cnt - 3))

                        def mk2(e, gslot=gslot, dti4=dti4, db=db, T=T, bank=bank):
                            hb = arB[:, gslot * 16384:(gslot + 1) * 16384].rearrange(
                                "p (k t) -> p k t", k=16)
                            wb = arC[:, gslot * 8192:(gslot + 1) * 8192].rearrange(
                                "p (k d) -> p k d", k=16)
                            dti = db * 4 + dti4
                            for k in range(16):
                                ins = e.matmul(bank[:],
                                               wb[:, k, dti4 * 128:(dti4 + 1) * 128],
                                               hb[:, k, T * 512:(T + 1) * 512],
                                               start=(k == 0), stop=(k == 15))
                            return ins
                        p.emit("te", mk2, waits=mw, inc="tF2")
                        o_ap = out2[:, dti * 1024 + T * 512: dti * 1024 + T * 512 + 512]
                        if g == 0:
                            p.emit("ve", lambda e, o_ap=o_ap, bank=bank: e.tensor_copy(o_ap, bank[:]),
                                   waits=[("tF2", p.val("tF2"))], inc="vF2")
                        else:
                            p.emit("ve", lambda e, o_ap=o_ap, bank=bank: e.tensor_tensor(
                                o_ap, o_ap, bank[:], op=ALU.add),
                                waits=[("tF2", p.val("tF2"))], inc="vF2")
        for dti in range(16):
            dglob = half * 16 + dti
            xslot = dglob % 2
            sem_xb = "dXb%d" % xslot
            xw = [("dY2", dY2_all), ("vFin", p.val("vFin"))]
            p.emit("sy", lambda e, dglob=dglob, xslot=xslot: e.dma_start(
                xfbuf[:, xslot * 1024:(xslot + 1) * 1024],
                y2_d[dglob * 128:(dglob + 1) * 128, :]),
                waits=xw, inc=sem_xb, inc_amt=16)
            xb_val = p.val(sem_xb)
            ya = y1t
            vw = [("vF2", p.val("vF2")), (sem_xb, xb_val),
                  ("dY3", 16 * dglob)]
            p.emit("ve", lambda e, ya=ya, dti=dti, dglob=dglob: e.tensor_scalar(
                ya, out2[:, dti * 1024:(dti + 1) * 1024], 0.5,
                bhalf[:, dglob:dglob + 1], op0=ALU.mult, op1=ALU.add),
                waits=vw + [("vC", vC_all)], inc="vFin")
            p.emit("ve", lambda e, ya=ya, xslot=xslot: e.tensor_tensor(
                ya, ya, xfbuf[:, xslot * 1024:(xslot + 1) * 1024], op=ALU.add),
                inc="vFin")
            p.emit("sy", lambda e, ya=ya, dglob=dglob: e.dma_start(
                y3_out[dglob * 128:(dglob + 1) * 128, :], ya),
                waits=[("vFin", p.val("vFin"))], inc="dY3", inc_amt=16)

    p.wait("sy", "dY3", p.val("dY3"))
    p.run()
    return nc


def _yarn_tables():
    dim, base, factor = 64, 10000.0, 128.0
    beta_fast, beta_slow, orig_max = 32.0, 1.0, 8192

    def corr_dim(r):
        return dim * math.log(orig_max / (r * 2 * math.pi)) / (2 * math.log(base))
    low = max(math.floor(corr_dim(beta_fast)), 0)
    high = min(math.ceil(corr_dim(beta_slow)), dim - 1)
    hi = high + 0.001 if low == high else float(high)
    ramp = np.clip((np.arange(dim // 2, dtype=np.float32) - low) / (hi - low), 0.0, 1.0)
    inv_freq_mask = 1.0 - ramp
    freq_extra = 1.0 / base ** (np.arange(0, dim, 2, dtype=np.float32) / dim)
    freq_inter = freq_extra / factor
    inv_freq = freq_inter * (1.0 - inv_freq_mask) + freq_extra * inv_freq_mask
    t = np.arange(S, dtype=np.float32)
    freqs = np.outer(t, inv_freq)
    return np.cos(freqs), np.sin(freqs)


def _deint_idx(n=64):
    return np.concatenate([np.arange(0, n, 2), np.arange(1, n, 2)])


def _core_chunks(c):
    return c, 15 - c


def prep_l1_maps(inputs):
    x = np.asarray(inputs["L0_attn_x"])[0]
    w_in = np.ascontiguousarray(np.asarray(inputs["L0_ffn_inter_w"]))
    b_in = np.asarray(inputs["L0_ffn_inter_b"])
    w_out = np.ascontiguousarray(np.asarray(inputs["L0_ffn_output_w"]))
    b_out = np.asarray(inputs["L0_ffn_output_b"])
    wq_a = np.asarray(inputs["L0_attn_wq_a"])
    wq_b = np.asarray(inputs["L0_attn_wq_b"])
    wkv_a = np.asarray(inputs["L0_attn_wkv_a"])
    qln = np.asarray(inputs["L0_attn_q_a_ln_w"])

    binT = np.ascontiguousarray(b_in.reshape(128, 128).T)
    boutT = np.ascontiguousarray(b_out.reshape(32, 128).T)
    wlnq = np.ascontiguousarray(qln.reshape(8, 128).T)
    wqaT = np.ascontiguousarray(wq_a.T)

    di = _deint_idx(64)
    idxq = np.empty(4096, np.int64)
    for h in range(H):
        idxq[h * 128:h * 128 + 64] = h * 128 + 64 + di
        idxq[h * 128 + 64:(h + 1) * 128] = h * 128 + np.arange(64)
    wqbT = np.ascontiguousarray(wq_b[idxq].T)

    idxkv = np.arange(320)
    idxkv[256:] = 256 + di
    wkvaT = np.ascontiguousarray(wkv_a[idxkv].T)

    cosS, sinS = _yarn_tables()
    iden = np.eye(128, dtype=np.float32)

    maps = []
    for c in range(NCORE):
        cA, cB = _core_chunks(c)
        tok = np.concatenate([np.arange(cA * CH, (cA + 1) * CH),
                              np.arange(cB * CH, (cB + 1) * CH)])
        maps.append({
            "x": np.ascontiguousarray(x[tok]),
            "w_in": w_in, "binT": binT, "w_out": w_out, "boutT": boutT,
            "wqaT": wqaT, "wqbT": wqbT, "wlnq": wlnq, "wkvaT": wkvaT,
            "cosT": np.ascontiguousarray(np.concatenate([cosS[tok].T, cosS[tok].T], 0)),
            "sinT": np.ascontiguousarray(np.concatenate([sinS[tok].T, sinS[tok].T], 0)),
            "iden": iden,
        })
    return maps


def prep_l2_maps(inputs, l1res):
    w_in = np.ascontiguousarray(np.asarray(inputs["L0_ffn_inter_w"]))
    b_in = np.asarray(inputs["L0_ffn_inter_b"])
    w_out = np.ascontiguousarray(np.asarray(inputs["L0_ffn_output_w"]))
    b_out = np.asarray(inputs["L0_ffn_output_b"])
    wkv_b = np.asarray(inputs["L0_attn_wkv_b"])
    kvln = np.asarray(inputs["L0_attn_kv_a_ln_w"])
    wo = np.asarray(inputs["L0_attn_wo"])

    binT = np.ascontiguousarray(b_in.reshape(128, 128).T)
    boutT = np.ascontiguousarray(b_out.reshape(32, 128).T)
    wkvbT = np.ascontiguousarray(wkv_b.T)
    wlnkv = np.ascontiguousarray(kvln.reshape(2, 128).T)
    woT = np.ascontiguousarray(wo.T)

    G = np.empty((320, S), np.float32)
    for j in range(16):
        c = j if j < 8 else 15 - j
        half = 0 if j < 8 else 1
        G[:, j * 512:(j + 1) * 512] = l1res[c]["contrib"][:, half * 512:(half + 1) * 512]

    triT = np.where(np.arange(512)[:, None] <= np.arange(512)[None, :],
                    np.float32(0.0), np.float32(NEG)).astype(np.float32)

    maps = []
    for c in range(NCORE):
        cA, cB = _core_chunks(c)
        vis = np.full(26, NEG, np.float32)
        vis[0] = -C_SHIFT
        for j in range(8):
            if j < cA:
                vis[1 + j] = -C_SHIFT
        vis[9] = -C_SHIFT
        for j in range(16):
            if j < cB and j != cA:
                vis[10 + j] = -C_SHIFT
            if j == cA:
                vis[10 + j] = -C_SHIFT
        vis[10 + cB] = NEG
        visb = np.ascontiguousarray(np.broadcast_to(vis[None, :], (128, 26)).astype(np.float32))
        maps.append({
            "y1F": l1res[c]["y1F"], "qF": l1res[c]["qF"],
            "G": G, "Gown": l1res[c]["contrib"],
            "visb": visb, "triT": triT,
            "wkvbT": wkvbT, "wlnkv": wlnkv, "woT": woT,
            "w_in": w_in, "binT": binT, "w_out": w_out, "boutT": boutT,
        })
    return maps


_L1 = [None]
_L2 = [None]


def run_l1(inputs):
    if _L1[0] is None:
        _L1[0] = build_l1()
    maps = prep_l1_maps(inputs)
    res = run_bass_kernel_spmd(_L1[0], maps, core_ids=list(range(NCORE)))
    return res.results


def run_l2(inputs, l1res):
    if _L2[0] is None:
        _L2[0] = build_l2()
    maps = prep_l2_maps(inputs, l1res)
    res = run_bass_kernel_spmd(_L2[0], maps, core_ids=list(range(NCORE)))
    return res.results


def kernel(**inputs):
    l1res = run_l1(inputs)
    l2res = run_l2(inputs, l1res)
    out = np.empty((1, S, D), np.float32)
    for c in range(NCORE):
        cA, cB = _core_chunks(c)
        y3 = l2res[c]["y3F"]
        out[0, cA * CH:(cA + 1) * CH] = y3[:, 0:512].T
        out[0, cB * CH:(cB + 1) * CH] = y3[:, 512:1024].T
    return out
